# version 02
# baseline (speedup 1.0000x reference)
import numpy as np
import concourse.bass as bass
import concourse.mybir as mybir

F32 = mybir.dt.float32
BF16 = mybir.dt.bfloat16
AF = mybir.ActivationFunctionType
ALU = mybir.AluOpType
AX = mybir.AxisListType

ENGINES = ("tensor", "vector", "scalar", "gpsimd", "sync")

L = 2048
D = 1024
C = 2048
NF = 2176
NFFT = 4096
INNER = 4096
M32 = float(3 << 22)
M64 = float(3 << 51)
EPS = 1e-5


def ts(i, n):
    return slice(i * n, (i + 1) * n)


class Prog:
    def __init__(self):
        self.q = {e: [] for e in ENGINES}
        self.cnt = {}
        self.handles = {}

    def sem(self, name):
        if name not in self.cnt:
            self.cnt[name] = 0
        return name

    def raw(self, eng, fn):
        self.q[eng].append(fn)

    def wait(self, eng, sem, val):
        if val is None or val <= 0 or sem not in self.cnt:
            return
        self.q[eng].append(lambda e, s=sem, v=val: e.wait_ge(self.handles[s], v))

    def wait_cur(self, eng, sem):
        if sem in self.cnt:
            self.wait(eng, sem, self.cnt[sem])

    def inc(self, eng, sem, by=1):
        self.sem(sem)
        self.cnt[sem] += by
        def f(e, s=sem, b=by):
            e.drain()
            e.nop().then_inc(self.handles[s], b)
        self.q[eng].append(f)
        return self.cnt[sem]

    def dma(self, eng, out_ap, in_ap, sem):
        self.sem(sem)
        before = self.cnt[sem]
        if before > 0:
            self.wait(eng, sem, before)
        self.cnt[sem] += 16
        def f(e, o=out_ap, i=in_ap, s=sem):
            e.dma_start(out=o, in_=i).then_inc(self.handles[s], 16)
        self.q[eng].append(f)
        return self.cnt[sem]

    def barrier(self):
        marks = {}
        for e in ("tensor", "vector", "scalar", "gpsimd"):
            marks[e] = self.inc(e, "bar_" + e)
        for e in ENGINES:
            for o in ("tensor", "vector", "scalar", "gpsimd"):
                if o != e:
                    self.wait(e, "bar_" + o, marks[o])
            for qn in list(self.cnt):
                if qn.startswith("q"):
                    self.wait_cur(e, qn)

    def emit(self, nc, stack):
        for name in self.cnt:
            self.handles[name] = stack.enter_context(nc.semaphore(name))
        block = stack.enter_context(nc.Block())
        def runner(lst):
            def _(eng):
                for f in lst:
                    f(eng)
            return _
        block.tensor(runner(self.q["tensor"]))
        block.vector(runner(self.q["vector"]))
        block.scalar(runner(self.q["scalar"]))
        block.gpsimd(runner(self.q["gpsimd"]))
        block.sync(runner(self.q["sync"]))


class MK:
    def __init__(self, nc, stack, n_sub=4, dbg=None):
        self.nc = nc
        self.p = Prog()
        self.E = stack.enter_context
        self.n_sub = n_sub
        self.dbg = dbg
        self.declare_dram()
        self.declare_sbuf()
        self.marks = {}

    def declare_dram(self):
        nc = self.nc
        d = lambda n, sh, dt=F32: nc.dram_tensor(n, list(sh), dt, kind="ExternalInput").ap()
        dI = lambda n, sh, dt: nc.dram_tensor(n, list(sh), dt, kind="Internal").ap()
        self.x0 = d("x0", [D, L])
        self.WinT = d("WinT", [D, 2 * C])
        self.conv_wh = d("conv_wh", [128, 64])
        self.conv_bh = d("conv_bh", [128, 16])
        self.k_ssm = d("k_ssm", [C, NFFT])
        self.D_skip = d("D_skip", [1, C])
        self.WoutT = d("WoutT", [C, D])
        self.WqkT = d("WqkT", [D, 2048])
        self.WvT = d("WvT", [D, 1024])
        self.WprojTp = d("WprojTp", [16 * 128, D])
        self.bprojh = d("bprojh", [128, 8])
        self.wgT = d("wgT", [D, INNER])
        self.wuT = d("wuT", [D, INNER])
        self.wdT = d("wdT", [INNER, D])
        self.Cf = d("Cf", [NFFT, NF])
        self.Sf = d("Sf", [NFFT, NF])
        self.Ci = d("Ci", [NF, L])
        self.Si = d("Si", [NF, L])
        self.ones_h = d("ones_h", [128, 128])
        self.ident_h = d("ident_h", [128, 128])
        self.augq_h = d("augq_h", [8, 128])
        self.augk_h = d("augk_h", [8, 128])

        self.xres = dI("xres", [D, L], F32)
        self.CfB = dI("CfB", [NFFT, NF], BF16)
        self.SfB = dI("SfB", [NFFT, NF], BF16)
        self.CiB = dI("CiB", [NF, L], BF16)
        self.SiB = dI("SiB", [NF, L], BF16)
        self.WinTB = dI("WinTB", [D, 2 * C], BF16)
        self.WoutTB = dI("WoutTB", [C, D], BF16)
        self.WqkTB = dI("WqkTB", [D, 2048], BF16)
        self.WvTB = dI("WvTB", [D, 1024], BF16)
        self.WprojTpB = dI("WprojTpB", [16 * 128, D], BF16)
        self.kssmB = dI("kssmB", [C, NFFT], BF16)
        self.kfB = dI("kfB", [2 * NF, C], F32)
        self.wqg = dI("wqg", [D, INNER], BF16)
        self.wqu = dI("wqu", [D, INNER], BF16)
        self.wqd = dI("wqd", [INNER, D], BF16)
        self.vscr = dI("vscr", [256, 1024], BF16)
        self.rowscr = dI("rowscr", [1, 2048], F32)
        self.dbgscr = dI("dbgscr", [128, 512], F32)
        self.out = nc.dram_tensor("out", [D, L], F32, kind="ExternalOutput").ap()

    def declare_sbuf(self):
        nc, E = self.nc, self.E
        sb = lambda n, sh, dt: E(nc.sbuf_tensor(n, list(sh), dt)).ap()
        self.BIG = sb("SB_BIG", [128, 16384], F32)
        self.WS = sb("SB_WS", [128, 24576], BF16)
        self.XN = sb("SB_XN", [128, 16384], BF16)
        self.WR = sb("SB_WR", [128, 8192], BF16)
        self.SMA = sb("SB_SMA", [128, 2048], F32)
        self.SMB = sb("SB_SMB", [128, 2048], F32)
        self.ONES = sb("ones_t", [128, 128], BF16)
        self.ONESF = sb("onesf_t", [128, 128], F32)
        self.IDT = sb("idt_t", [128, 128], BF16)
        self.IDTF = sb("idtf_t", [128, 128], F32)
        self.RA = sb("rowa_t", [1, 2048], F32)
        self.RB = sb("rowb_t", [1, 2048], F32)
        self.RC = sb("rowc_t", [1, 2048], F32)
        self.COLS = sb("cols_t", [128, 32], F32)
        self.CW = sb("convw_t", [128, 64], F32)
        self.CB = sb("convb_t", [128, 16], F32)
        self.BPC = sb("bpc_t", [128, 8], F32)
        self.AUGQ = sb("augq_t", [8, 128], BF16)
        self.AUGK = sb("augk_t", [8, 128], BF16)
        self.DROW = sb("drow_t", [128, 2048], BF16)
        ps = lambda n, sh, dt: E(nc.psum_tensor(n, list(sh), dt)).ap()
        self.pb = [ps(f"pb{i}", [128, 512], F32) for i in range(6)]
        self.pt = [ps(f"pt{i}", [128, 512], BF16) for i in range(2)]


    def col(self, i):
        return self.COLS[:, i:i + 1]

    def v(self, fn):
        self.p.raw("vector", lambda e, f=fn: (e.drain(), f(e)))

    def s(self, fn):
        self.p.raw("scalar", lambda e, f=fn: (e.drain(), f(e)))

    def t(self, fn):
        self.p.raw("tensor", fn)

    def mark(self, name, eng, sem):
        self.marks[name] = (sem, self.p.inc(eng, sem))

    def wait_mark(self, eng, name):
        if name in self.marks:
            s, v = self.marks[name]
            self.p.wait(eng, s, v)

    def round_into(self, out_ap, in_ap):
        self.v(lambda e, o=out_ap, i=in_ap: e.tensor_scalar(
            o, i, self.col(1), self.col(2), ALU.add, ALU.add))

    def tile_dma(self, eng, dst2d, dram_slice, qn):
        m = dram_slice.shape[1]
        src3 = dram_slice.rearrange("(a p) m -> p a m", p=128)
        dst3 = dst2d.rearrange("p (a m) -> p a m", m=m)
        return self.p.dma(eng, dst3, src3, qn)

    def ph_init(self):
        p = self.p
        for dst, src in ((self.CfB, self.Cf), (self.SfB, self.Sf),
                         (self.CiB, self.Ci), (self.SiB, self.Si),
                         (self.WinTB, self.WinT), (self.WoutTB, self.WoutT),
                         (self.WqkTB, self.WqkT), (self.WvTB, self.WvT),
                         (self.WprojTpB, self.WprojTp), (self.kssmB, self.k_ssm)):
            p.dma("gpsimd", dst, src, "qCV")
        p.dma("gpsimd", self.ONES[:, :], self.ones_h, "qG")
        p.dma("gpsimd", self.IDT[:, :], self.ident_h, "qG")
        p.dma("sync", self.CW[:, :], self.conv_wh, "qS")
        p.dma("gpsimd", self.AUGQ[:, :], self.augq_h, "qG")
        p.dma("gpsimd", self.AUGK[:, :], self.augk_h, "qG")
        dsk_row = self.WS[0:1, 0:2048]
        p.dma("gpsimd", dsk_row, self.D_skip, "qG")
        p.dma("sync", self.CB[:, :], self.conv_bh, "qS")
        p.dma("sync", self.BPC[:, :], self.bprojh, "qS")
        p.dma("sync", self.ONESF[:, :], self.ones_h, "qS")
        p.dma("sync", self.IDTF[:, :], self.ident_h, "qS")
        p.wait_cur("vector", "qG")
        p.wait_cur("vector", "qS")
        self.v(lambda e: e.memset(self.col(0), 1.0))
        self.v(lambda e: e.tensor_scalar(self.col(0), self.col(0), float(2 ** 30), float(2 ** 30), ALU.add, ALU.subtract))
        self.v(lambda e: e.tensor_scalar(self.col(1), self.col(0), M64, M32, ALU.mult, ALU.add))
        self.v(lambda e: e.tensor_scalar(self.col(2), self.col(1), -1.0, None, ALU.mult))
        self.v(lambda e: e.memset(self.col(3), 1.0))
        self.mark("init_v", "vector", "vS")
        p.wait_cur("tensor", "qG")
        self.wait_mark("tensor", "init_v")
        for nt in range(4):
            self.t(lambda e, nt=nt: e.matmul(self.pb[nt][:, :], self.ONES[0:1, :], dsk_row[:, ts(nt, 512)], start=True, stop=True))
        self.mark("drow_mm", "tensor", "tS")
        self.wait_mark("vector", "drow_mm")
        for nt in range(4):
            self.v(lambda e, nt=nt: e.tensor_copy(self.DROW[:, ts(nt, 512)], self.pb[nt][:, :]))
        p.barrier()

    def ph_ternarize(self):
        p = self.p
        mats = ((self.wgT, self.wqg, 8, 4096), (self.wuT, self.wqu, 8, 4096),
                (self.wdT, self.wqd, 32, 1024))
        for mi, (W, WQ, nch, ncol) in enumerate(mats):
            nsub = ncol // 2048 if ncol >= 2048 else 1
            subw = min(ncol, 2048)
            self.v(lambda e: e.memset(self.col(4), 0.0))
            for ch in range(nch):
                slot = ch % 2
                qn = f"qX{slot}"
                xr = self.BIG[:, slot * ncol:(slot + 1) * ncol]
                self.wait_mark("sync", f"xr{slot}")
                p.dma("sync", xr, W[ts(ch, 128), :], qn)
                p.wait_cur("vector", qn)
                self.v(lambda e, xr=xr: e.tensor_reduce(self.col(5), xr, AX.X, ALU.add, apply_absolute_value=True))
                self.v(lambda e: e.tensor_tensor(self.col(4), self.col(4), self.col(5), ALU.add))
                self.mark(f"xr{slot}", "vector", "vS")
            self.mark("tern_abs", "vector", "vS")
            self.wait_mark("tensor", "tern_abs")
            self.t(lambda e: e.matmul(self.pb[0][0:1, 0:1], self.col(3), self.col(4), start=True, stop=True))
            self.mark("tern_tot", "tensor", "tS")
            self.wait_mark("vector", "tern_tot")
            nel = float(1024 * 4096)
            self.v(lambda e: e.tensor_scalar(self.RA[:, 0:1], self.pb[0][0:1, 0:1], 1.0 / nel, EPS, ALU.mult, ALU.max))
            self.v(lambda e: e.reciprocal(self.RA[:, 0:1], self.RA[:, 0:1]))
            self.mark("tern_row", "vector", "vS")
            self.wait_mark("tensor", "tern_row")
            self.t(lambda e: e.matmul(self.pb[0][:, 0:1], self.ONESF[0:1, :], self.RA[:, 0:1], start=True, stop=True))
            self.mark("tern_bc", "tensor", "tS")
            self.wait_mark("vector", "tern_bc")
            self.v(lambda e: e.tensor_copy(self.col(6), self.pb[0][:, 0:1]))
            self.v(lambda e, mi=mi: e.reciprocal(self.col(7 + mi), self.col(6)))
            self.mark("tern_s", "vector", "vS")
            for ch in range(nch):
                slot = ch % 2
                qn = f"qX{slot}"
                xr = self.BIG[:, slot * ncol:(slot + 1) * ncol]
                self.wait_mark("sync", f"xr{slot}")
                p.dma("sync", xr, W[ts(ch, 128), :], qn)
                p.wait_cur("vector", qn)
                p.wait_cur("vector", f"qW{slot}")
                for su in range(nsub):
                    xrs = xr[:, su * subw:(su + 1) * subw]
                    sma = self.SMA[:, 0:subw]
                    wsd = self.WS[:, slot * 4096 + su * subw: slot * 4096 + (su + 1) * subw]
                    self.v(lambda e, xrs=xrs, sma=sma: e.tensor_scalar(sma, xrs, self.col(6), None, ALU.mult))
                    self.round_into(sma, sma)
                    self.v(lambda e, sma=sma, wsd=wsd: e.tensor_scalar(wsd, sma, -1.0, 1.0, ALU.max, ALU.min))
                self.mark(f"xr{slot}", "vector", "vS")
                self.mark(f"tern_q{ch}", "vector", "vS")
                self.wait_mark("sync", f"tern_q{ch}")
                p.dma("sync", WQ[ts(ch, 128), :], self.WS[:, slot * 4096: slot * 4096 + ncol], f"qW{slot}")
        p.barrier()

    def kt_ap(self, lc, csl):
        if lc < 24:
            base = self.WS[:, lc * 1024:(lc + 1) * 1024]
        else:
            base = self.XN[:, (lc - 24) * 1024:(lc - 24) * 1024 + 1024]
        return base[:, csl]

    def ph_kf(self):
        p = self.p
        p.wait_cur("sync", "qCV")
        p.wait_cur("tensor", "qCV")
        for half in range(2):
            self.wait_mark("vector", "kfmm0")
            self.wait_mark("vector", "kfmm1")
            for cc in range(8):
                ch = half * 8 + cc
                slot = cc % 2
                qn = f"qK{slot}"
                kb = self.XN[:, 8192 + slot * 4096: 8192 + (slot + 1) * 4096]
                self.wait_mark("sync", f"kb{slot}")
                p.dma("sync", kb, self.kssmB[ts(ch, 128), :], qn)
                p.wait_cur("tensor", qn)
                for lg in range(8):
                    bank = self.pt[lg % 2]
                    self.wait_mark("tensor", f"ptf{lg % 2}")
                    for k in range(4):
                        lc = lg * 4 + k
                        self.t(lambda e, kb=kb, lc=lc, bank=bank, k=k: e.transpose(
                            bank[:, ts(k, 128)], kb[:, ts(lc, 128)], self.IDT[:, :]))
                    self.mark(f"ptm{lg % 2}", "tensor", "tS")
                    self.wait_mark("vector", f"ptm{lg % 2}")
                    for k in range(4):
                        lc = lg * 4 + k
                        self.v(lambda e, lc=lc, cc=cc, bank=bank, k=k: e.tensor_copy(
                            self.kt_ap(lc, ts(cc, 128)), bank[:, ts(k, 128)]))
                    self.mark(f"ptf{lg % 2}", "vector", "vS")
                self.mark(f"kb{slot}", "vector", "vS")
            self.wait_mark("tensor", "ptf0")
            self.wait_mark("tensor", "ptf1")
            for fc in range(17):
                for ri, CS in enumerate((self.CfB, self.SfB)):
                    pr = (fc * 2 + ri) % 2
                    self.wait_mark("sync", f"kw{pr}")
                    for hl in range(2):
                        self.tile_dma("sync", self.WR[:, (2 * pr + hl) * 2048:(2 * pr + hl + 1) * 2048],
                                      CS[hl * 2048:(hl + 1) * 2048, ts(fc, 128)], f"qL{pr}")
                    p.wait_cur("tensor", f"qL{pr}")
                    b0, b1 = self.pb[2 * ri], self.pb[2 * ri + 1]
                    self.wait_mark("tensor", f"kfev{ri}")
                    for lc in range(32):
                        lt = self.WR[:, (2 * pr + lc // 16) * 2048 + (lc % 16) * 128:][:, 0:128]
                        for ct, bank in ((0, b0), (1, b1)):
                            self.t(lambda e, lt=lt, lc=lc, ct=ct, bank=bank, half=half: e.matmul(
                                bank[:, :], lt, self.kt_ap(lc, ts(ct, 512)),
                                start=(lc == 0), stop=(lc == 31)))
                    self.mark(f"kw{pr}", "tensor", "tS")
                    self.mark(f"kfmm{ri}", "tensor", "tS")
                    self.wait_mark("scalar", f"kfmm{ri}")
                    for ct, bank in ((0, b0), (1, b1)):
                        eslot = (fc * 4 + ri * 2 + ct) % 4
                        evt = self.SMA if eslot < 2 else self.SMB
                        ev = evt[:, (eslot % 2) * 512:(eslot % 2 + 1) * 512]
                        self.p.wait_cur("scalar", f"qE{eslot}")
                        self.s(lambda e, ev=ev, bank=bank: e.copy(ev, bank[:, :]))
                        self.mark(f"kfec{eslot}", "scalar", "sS")
                        self.wait_mark("sync", f"kfec{eslot}")
                        p.dma("sync", self.kfB[ri * NF + fc * 128: ri * NF + (fc + 1) * 128,
                                               half * 1024 + ct * 512: half * 1024 + (ct + 1) * 512],
                              ev, f"qE{eslot}")
                    self.mark(f"kfev{ri}", "scalar", "sS")
        p.barrier()

    def ph_ln_std(self, xsrc, li):
        p = self.p
        for cc in range(8):
            qn = f"qX{cc % 2}"
            xr = self.BIG[:, ts(cc, 2048)]
            self.wait_mark("sync", "big_free")
            p.dma("sync", xr, xsrc[ts(cc, 128), :], qn)
            p.wait_cur("scalar", qn)
            self.s(lambda e, xr=xr, cc=cc: e.copy(self.XN[:, ts(cc, 2048)], xr))
            self.s(lambda e, cc=cc: e.activation(self.WS[:, ts(cc, 2048)], self.XN[:, ts(cc, 2048)], AF.Square))
        self.mark(f"ln{li}_sq", "scalar", "sS")
        self.wait_mark("tensor", f"ln{li}_sq")
        for qi, src in ((0, self.XN), (1, self.WS)):
            for nt in range(4):
                bank = self.pb[(qi * 4 + nt) % 2]
                self.wait_mark("tensor", f"lnb{(qi * 4 + nt) % 2}")
                for cc in range(8):
                    self.t(lambda e, src=src, cc=cc, nt=nt, bank=bank: e.matmul(
                        bank[0:1, :], self.ONES[:, 0:1], src[:, cc * 2048 + nt * 512: cc * 2048 + (nt + 1) * 512],
                        start=(cc == 0), stop=(cc == 7)))
                self.mark(f"lnmm", "tensor", "tS")
                self.wait_mark("vector", f"lnmm")
                self.v(lambda e, qi=qi, nt=nt, bank=bank: e.tensor_copy(
                    (self.RA if qi == 0 else self.RB)[:, nt * 512: (nt + 1) * 512], bank[0:1, :]))
                self.mark(f"lnb{(qi * 4 + nt) % 2}", "vector", "vS")
        self.v(lambda e: e.tensor_scalar(self.RA[:, :], self.RA[:, :], 1.0 / D, None, ALU.mult))
        self.mark("ln_r0", "vector", "vS")
        self.wait_mark("scalar", "ln_r0")
        self.s(lambda e: e.activation(self.RC[:, :], self.RA[:, :], AF.Square))
        self.mark("ln_r3", "scalar", "sS")
        self.wait_mark("vector", "ln_r3")
        self.v(lambda e: e.scalar_tensor_tensor(self.RB[:, :], self.RB[:, :], 1.0 / D, self.RC[:, :], ALU.mult, ALU.subtract))
        self.mark("ln_var", "vector", "vS")
        self.v(lambda e: e.tensor_scalar(self.RB[:, :], self.RB[:, :], EPS, None, ALU.add))
        self.mark("ln_var", "vector", "vS")
        self.wait_mark("scalar", "ln_var")
        self.s(lambda e: e.activation(self.RB[:, :], self.RB[:, :], AF.Sqrt))
        self.mark("ln_sd", "scalar", "sS")
        self.wait_mark("vector", "ln_sd")
        self.v(lambda e: e.reciprocal(self.RB[:, :], self.RB[:, :]))
        self.v(lambda e: e.tensor_tensor(self.RA[:, :], self.RA[:, :], self.RB[:, :], ALU.mult))
        self.mark("ln_rows", "vector", "vS")
        self.wait_mark("tensor", "ln_rows")
        for bi, (rsrc, dst) in enumerate(((1, self.SMA), (2, self.SMB))):
            R = self.RB if bi == 0 else self.RA
            for nt in range(4):
                bank = self.pb[2 + (bi * 4 + nt) % 2]
                self.wait_mark("tensor", f"lnc{(bi * 4 + nt) % 2}")
                self.t(lambda e, R=R, nt=nt, bank=bank: e.matmul(
                    bank[:, :], self.ONESF[0:1, :], R[:, nt * 512:(nt + 1) * 512], start=True, stop=True))
                self.mark("lnbc", "tensor", "tS")
                self.wait_mark("vector", "lnbc")
                self.v(lambda e, dst=dst, nt=nt, bank=bank: e.tensor_copy(dst[:, ts(nt, 512)], bank[:, :]))
                self.mark(f"lnc{(bi * 4 + nt) % 2}", "vector", "vS")
        for cc in range(8):
            big = self.BIG[:, ts(cc, 2048)]
            self.v(lambda e, big=big: e.tensor_tensor(big, big, self.SMA[:, :], ALU.mult))
            self.v(lambda e, big=big, cc=cc: e.tensor_tensor(self.XN[:, ts(cc, 2048)], big, self.SMB[:, :], ALU.subtract))
        self.mark(f"ln{li}_done", "vector", "vS")
        self.p.barrier()

    def ph_ln_quant(self, xsrc, li):
        p = self.p
        for cc in range(8):
            qn = f"qX{cc % 2}"
            xr = self.BIG[:, ts(cc, 2048)]
            self.wait_mark("sync", "big_free")
            p.dma("sync", xr, xsrc[ts(cc, 128), :], qn)
            p.wait_cur("scalar", qn)
            self.s(lambda e, xr=xr, cc=cc: e.copy(self.XN[:, ts(cc, 2048)], xr))
            self.s(lambda e, cc=cc: e.activation(self.WS[:, ts(cc, 2048)], self.XN[:, ts(cc, 2048)], AF.Square))
        self.mark(f"ln{li}_sq", "scalar", "sS")
        self.wait_mark("tensor", f"ln{li}_sq")
        for qi, src in ((0, self.XN), (1, self.WS)):
            for nt in range(4):
                bank = self.pb[(qi * 4 + nt) % 2]
                self.wait_mark("tensor", f"lnb{(qi * 4 + nt) % 2}")
                for cc in range(8):
                    self.t(lambda e, src=src, cc=cc, nt=nt, bank=bank: e.matmul(
                        bank[0:1, :], self.ONES[:, 0:1], src[:, cc * 2048 + nt * 512: cc * 2048 + (nt + 1) * 512],
                        start=(cc == 0), stop=(cc == 7)))
                self.mark(f"lnmm", "tensor", "tS")
                self.wait_mark("vector", f"lnmm")
                self.v(lambda e, qi=qi, nt=nt, bank=bank: e.tensor_copy(
                    (self.RA if qi == 0 else self.RB)[:, nt * 512: (nt + 1) * 512], bank[0:1, :]))
                self.mark(f"lnb{(qi * 4 + nt) % 2}", "vector", "vS")
        self.v(lambda e: e.tensor_scalar(self.RA[:, :], self.RA[:, :], 1.0 / D, None, ALU.mult))
        self.mark("ln_r0", "vector", "vS")
        self.wait_mark("scalar", "ln_r0")
        self.s(lambda e: e.activation(self.RC[:, :], self.RA[:, :], AF.Square))
        self.mark("ln_r3", "scalar", "sS")
        self.wait_mark("vector", "ln_r3")
        self.v(lambda e: e.scalar_tensor_tensor(self.RB[:, :], self.RB[:, :], 1.0 / D, self.RC[:, :], ALU.mult, ALU.subtract))
        self.mark("ln_var", "vector", "vS")
        self.v(lambda e: e.tensor_scalar(self.RB[:, :], self.RB[:, :], EPS, None, ALU.add))
        self.mark("ln_var", "vector", "vS")
        self.wait_mark("scalar", "ln_var")
        self.s(lambda e: e.activation(self.RB[:, :], self.RB[:, :], AF.Sqrt))
        self.mark("ln_sd", "scalar", "sS")
        self.wait_mark("vector", "ln_sd")
        self.v(lambda e: e.reciprocal(self.RB[:, :], self.RB[:, :]))
        self.mark("ln_rows", "vector", "vS")
        self.wait_mark("tensor", "ln_rows")
        for nt in range(4):
            bank = self.pb[2 + nt % 2]
            self.wait_mark("tensor", f"lnc{nt % 2}")
            self.t(lambda e, nt=nt, bank=bank: e.matmul(
                bank[:, :], self.ONESF[0:1, :], self.RA[:, nt * 512:(nt + 1) * 512], start=True, stop=True))
            self.mark("lnbc", "tensor", "tS")
            self.wait_mark("vector", "lnbc")
            self.v(lambda e, nt=nt, bank=bank: e.tensor_copy(self.SMB[:, ts(nt, 512)], bank[:, :]))
            self.mark(f"lnc{nt % 2}", "vector", "vS")
        for cc in range(8):
            big = self.BIG[:, ts(cc, 2048)]
            self.v(lambda e, big=big: e.tensor_tensor(big, big, self.SMB[:, :], ALU.subtract))
            if cc == 0:
                self.v(lambda e, big=big: e.tensor_copy(self.SMA[:, :], big))
            else:
                self.v(lambda e, big=big: e.tensor_tensor(self.SMA[:, :], self.SMA[:, :], big, ALU.max))
        for cc in range(8):
            big = self.BIG[:, ts(cc, 2048)]
            if cc == 0:
                self.v(lambda e, big=big: e.tensor_copy(self.SMB[:, :], big))
            else:
                self.v(lambda e, big=big: e.tensor_tensor(self.SMB[:, :], self.SMB[:, :], big, ALU.min))
        self.v(lambda e: e.tensor_scalar(self.SMB[:, :], self.SMB[:, :], -1.0, None, ALU.mult))
        self.v(lambda e: e.tensor_tensor(self.SMA[:, :], self.SMA[:, :], self.SMB[:, :], ALU.max))
        self.mark("lnq_tree", "vector", "vS")
        self.wait_mark("tensor", "lnq_tree")
        for tt in range(16):
            bank = self.pb[4 + tt % 2]
            self.wait_mark("tensor", f"lnt{tt % 2}")
            self.t(lambda e, tt=tt, bank=bank: e.transpose(
                bank[:, 0:128], self.SMA[:, ts(tt, 128)], self.IDTF[:, :]))
            self.mark("lntr", "tensor", "tS")
            self.wait_mark("vector", "lntr")
            self.v(lambda e, tt=tt, bank=bank: e.tensor_reduce(
                self.SMB[:, tt:tt + 1], bank[:, 0:128], AX.X, ALU.max, apply_absolute_value=True))
            self.mark(f"lnt{tt % 2}", "vector", "vS")
        self.mark("lnq_maxcol", "vector", "vS")
        self.wait_mark("tensor", "lnq_maxcol")
        self.t(lambda e: e.transpose(self.pb[4][0:16, 0:128], self.SMB[:, 0:16], self.IDTF[:, :]))
        self.mark("lnq_mt", "tensor", "tS")
        self.wait_mark("vector", "lnq_mt")
        self.v(lambda e: e.tensor_copy(self.SMB[0:16, 1792:1920], self.pb[4][0:16, 0:128]))
        self.mark("lnq_mt2", "vector", "vS")
        self.wait_mark("sync", "lnq_mt2")
        p.dma("sync", self.rowscr.rearrange("o (n p) -> (o n) p", p=128), self.SMB[0:16, 1792:1920], "qR")
        p.dma("sync", self.RA[:, :], self.rowscr, "qR")
        p.wait_cur("vector", "qR")
        self.v(lambda e: e.tensor_tensor(self.RC[:, :], self.RB[:, :], self.RA[:, :], ALU.mult))
        self.v(lambda e: e.tensor_scalar(self.RC[:, :], self.RC[:, :], 1.0 / 127.0, None, ALU.mult))
        self.v(lambda e: e.reciprocal(self.RA[:, :], self.RA[:, :]))
        self.v(lambda e: e.tensor_scalar(self.RA[:, :], self.RA[:, :], 127.0, None, ALU.mult))
        self.mark("lnq_rows", "vector", "vS")
        self.wait_mark("tensor", "lnq_rows")
        for nt in range(4):
            bank = self.pb[2 + nt % 2]
            self.wait_mark("tensor", f"lnc{nt % 2}")
            self.t(lambda e, nt=nt, bank=bank: e.matmul(
                bank[:, :], self.ONESF[0:1, :], self.RA[:, nt * 512:(nt + 1) * 512], start=True, stop=True))
            self.mark("lnbc", "tensor", "tS")
            self.wait_mark("vector", "lnbc")
            self.v(lambda e, nt=nt, bank=bank: e.tensor_copy(self.SMA[:, ts(nt, 512)], bank[:, :]))
            self.mark(f"lnc{nt % 2}", "vector", "vS")
        for cc in range(8):
            big = self.BIG[:, ts(cc, 2048)]
            self.v(lambda e, big=big: e.tensor_tensor(big, big, self.SMA[:, :], ALU.mult))
            self.round_into(self.XN[:, ts(cc, 2048)], big)
        self.mark(f"ln{li}_done", "vector", "vS")
        self.p.barrier()

    def ph_attn_a(self, xsrc):
        p = self.p
        WS = self.WS
        Zb = 16384

        def zsl(fc, ri):
            if fc == 16 and ri == 0:
                return self.WR[:, 6272:6528]
            assert fc < 16 or (fc == 16 and ri == 0)
            s = fc * 2 + ri
            return WS[:, Zb + s * 256: Zb + s * 256 + 256]

        for half in range(2):
            self.v(lambda e: e.memset(self.WR[:, 4096:4160], 0.0))
            self.mark("acpad", "vector", "vS")
            self.wait_mark("tensor", "acpad")
            for cc in range(8):
                ch = half * 8 + cc
                pr = cc % 2
                self.wait_mark("sync", f"awr{pr}")
                self.tile_dma("sync", self.WR[:, ts(pr, 1024)], self.WinTB[:, ts(ch, 128)], f"qL{pr}")
                p.wait_cur("tensor", f"qL{pr}")
                for lt in range(4):
                    bank = self.pb[lt % 2]
                    self.wait_mark("tensor", f"apb{lt % 2}")
                    for kc in range(8):
                        self.t(lambda e, pr=pr, kc=kc, lt=lt, bank=bank: e.matmul(
                            bank[:, :], self.WR[:, pr * 1024 + kc * 128: pr * 1024 + (kc + 1) * 128],
                            self.XN[:, kc * 2048 + lt * 512: kc * 2048 + (lt + 1) * 512],
                            start=(kc == 0), stop=(kc == 7)))
                    self.mark("axc", "tensor", "tS")
                    self.wait_mark("scalar", "axc")
                    self.wait_mark("scalar", "aconv_rd")
                    self.s(lambda e, lt=lt, bank=bank: e.copy(
                        self.WR[:, 4160 + lt * 512: 4160 + (lt + 1) * 512], bank[:, :]))
                    self.mark(f"apb{lt % 2}", "scalar", "sS")
                self.mark("aconv_wr", "scalar", "sS")
                self.mark(f"awr{pr}", "tensor", "tS")
                self.wait_mark("vector", "adiag_rd")
                for j in range(4):
                    self.v(lambda e, ch=ch, j=j: e.tensor_scalar(
                        self.WR[:, 6208 + j * 128: 6208 + (j + 1) * 128], self.IDT[:, :],
                        self.CW[:, ch * 4 + j: ch * 4 + j + 1], None, ALU.mult))
                self.mark("adiag", "vector", "vS")
                self.wait_mark("tensor", "adiag")
                self.wait_mark("tensor", "aconv_wr")
                for lt in range(4):
                    bank = self.pb[2 + lt % 2]
                    self.wait_mark("tensor", f"acb{lt % 2}")
                    for j in range(4):
                        off = 4160 + lt * 512 - 3 + j
                        self.t(lambda e, j=j, off=off, bank=bank: e.matmul(
                            bank[:, :], self.WR[:, 6208 + j * 128: 6208 + (j + 1) * 128],
                            self.WR[:, off: off + 512], start=(j == 0), stop=(j == 3)))
                    self.mark("aconvmm", "tensor", "tS")
                    st = self.WR[:, 6720 + (lt % 2) * 512: 6720 + (lt % 2 + 1) * 512]
                    self.wait_mark("scalar", "aconvmm")
                    self.wait_mark("scalar", f"ast{lt % 2}")
                    self.s(lambda e, bank=bank, st=st, ch=ch: e.activation(
                        st, bank[:, :], AF.Silu, bias=self.CB[:, ch:ch + 1]))
                    self.mark(f"acb{lt % 2}", "scalar", "sS")
                    self.mark("asilu", "scalar", "sS")
                    ptb = self.pt[lt % 2]
                    self.wait_mark("tensor", "asilu")
                    self.wait_mark("tensor", f"aptf{lt % 2}")
                    for k in range(4):
                        self.t(lambda e, st=st, k=k, ptb=ptb: e.transpose(
                            ptb[:, ts(k, 128)], st[:, ts(k, 128)], self.IDT[:, :]))
                    self.mark(f"aptm{lt % 2}", "tensor", "tS")
                    self.mark(f"ast{lt % 2}", "tensor", "tS")
                    self.wait_mark("vector", f"aptm{lt % 2}")
                    for k in range(4):
                        lc = lt * 4 + k
                        self.v(lambda e, lc=lc, cc=cc, ptb=ptb, k=k: e.tensor_copy(
                            WS[:, lc * 1024 + cc * 128: lc * 1024 + (cc + 1) * 128], ptb[:, ts(k, 128)]))
                    self.mark(f"aptf{lt % 2}", "vector", "vS")
                self.mark("aconv_rd", "tensor", "tS")
                self.mark("adiag_rd", "tensor", "tS")
            if self.dbg == "xt0" and half == 0:
                self.p.barrier(); self.dump_sbuf_bf16(self.WS[:, 0:16384]); return
            for sl in range(4):
                self.wait_mark("vector", "awb0")
                self.wait_mark("vector", "awb1")
                self.wait_mark("vector", "aymm")
                for fc in range(17):
                    for ri, CS in enumerate((self.CfB, self.SfB)):
                        pr = ri
                        self.wait_mark("sync", f"afw{pr}")
                        self.wait_mark("sync", "awr0")
                        self.wait_mark("sync", "awr1")
                        self.wait_mark("sync", "ainv")
                        self.tile_dma("sync", self.WR[:, ts(pr, 2048)], CS[0:2048, ts(fc, 128)], f"qF{pr}")
                        p.wait_cur("tensor", f"qF{pr}")
                        bank = self.pb[ri]
                        self.wait_mark("tensor", f"afb{ri}")
                        for lc in range(16):
                            self.t(lambda e, pr=pr, lc=lc, sl=sl, bank=bank: e.matmul(
                                bank[:, 0:256], self.WR[:, pr * 2048 + lc * 128: pr * 2048 + (lc + 1) * 128],
                                WS[:, lc * 1024 + sl * 256: lc * 1024 + (sl + 1) * 256],
                                start=(lc == 0), stop=(lc == 15)))
                        self.mark(f"afw{pr}", "tensor", "tS")
                        self.mark(f"afmm{ri}", "tensor", "tS")
                    kslot = fc % 2
                    self.wait_mark("sync", f"akf{kslot}")
                    for ri in range(2):
                        p.dma("sync", self.SMB[:, kslot * 1024 + ri * 256: kslot * 1024 + (ri + 1) * 256],
                              self.kfB[ri * NF + fc * 128: ri * NF + (fc + 1) * 128,
                                       half * 1024 + sl * 256: half * 1024 + (sl + 1) * 256], f"qKF{kslot}")
                    p.wait_cur("scalar", f"qKF{kslot}")
                    self.wait_mark("scalar", "afmm0")
                    self.wait_mark("scalar", "afmm1")
                    self.wait_mark("scalar", "axv_done")
                    xre = self.SMA[:, 0:256]
                    xim = self.SMA[:, 256:512]
                    self.s(lambda e, xre=xre: e.copy(xre, self.pb[0][:, 0:256]))
                    self.s(lambda e, xim=xim: e.copy(xim, self.pb[1][:, 0:256]))
                    self.mark("axev", "scalar", "sS")
                    self.mark("afb0", "scalar", "sS")
                    self.mark("afb1", "scalar", "sS")
                    self.wait_mark("vector", "axev")
                    kre = self.SMB[:, kslot * 1024 + 0: kslot * 1024 + 256]
                    kim = self.SMB[:, kslot * 1024 + 256: kslot * 1024 + 512]
                    tmp = self.SMA[:, 512:768]
                    tmp2 = self.SMA[:, 768:1024]
                    self.v(lambda e, tmp2=tmp2, xre=xre, kre=kre: e.tensor_tensor(tmp2, xre, kre, ALU.mult))
                    self.v(lambda e, tmp=tmp, xim=xim, kim=kim: e.tensor_tensor(tmp, xim, kim, ALU.mult))
                    self.v(lambda e, fc=fc, tmp=tmp, tmp2=tmp2: e.tensor_tensor(zsl(fc, 0), tmp2, tmp, ALU.subtract))
                    if fc < 16:
                        self.v(lambda e, tmp2=tmp2, xre=xre, kim=kim: e.tensor_tensor(tmp2, xre, kim, ALU.mult))
                        self.v(lambda e, tmp=tmp, xim=xim, kre=kre: e.tensor_tensor(tmp, xim, kre, ALU.mult))
                        self.v(lambda e, fc=fc, tmp=tmp, tmp2=tmp2: e.tensor_tensor(zsl(fc, 1), tmp2, tmp, ALU.add))
                    self.mark(f"akf{kslot}", "vector", "vS")
                    self.mark("axv_done", "vector", "vS")
                self.mark("aZ", "vector", "vS")
                self.wait_mark("sync", "azw")
                self.tile_dma("sync", self.WR[:, 4224:6272],
                              self.WinTB[:, C + half * 1024 + sl * 256: C + half * 1024 + (sl + 1) * 256], "qZW")
                self.wait_mark("tensor", "aZ")
                p.wait_cur("tensor", "qZW")
                for lc in range(16):
                    self.wait_mark("sync", "ainv")
                    self.wait_mark("sync", "afw0")
                    self.wait_mark("sync", "afw1")
                    self.tile_dma("sync", self.WR[:, 0:2176], self.CiB[0:NF, ts(lc, 128)], "qI0")
                    self.tile_dma("sync", self.WR[:, 2176:4224], self.SiB[0:2048, ts(lc, 128)], "qI1")
                    p.wait_cur("tensor", "qI0")
                    p.wait_cur("tensor", "qI1")
                    ybank = self.pb[2 + lc % 2]
                    self.wait_mark("tensor", f"ayb{lc % 2}")
                    nmm = 0
                    for ri in range(2):
                        for fc in range(17 if ri == 0 else 16):
                            base = fc * 128 if ri == 0 else 2176 + fc * 128
                            self.t(lambda e, base=base, fc=fc, ri=ri, ybank=ybank, first=(nmm == 0), last=(nmm == 32): e.matmul(
                                ybank[:, 0:256], self.WR[:, base: base + 128],
                                zsl(fc, ri), start=first, stop=last))
                            nmm += 1
                    self.mark("ainv", "tensor", "tS")
                    self.mark("aymm", "tensor", "tS")
                    zbank = self.pb[4 + lc % 2]
                    self.wait_mark("tensor", f"azb{lc % 2}")
                    for kc in range(8):
                        self.t(lambda e, kc=kc, lc=lc, zbank=zbank: e.matmul(
                            zbank[:, 0:256], self.XN[:, kc * 2048 + lc * 128: kc * 2048 + (lc + 1) * 128],
                            self.WR[:, 4224 + kc * 256: 4224 + (kc + 1) * 256],
                            start=(kc == 0), stop=(kc == 7)))
                    self.mark("azmm", "tensor", "tS")
                    zst = self.SMA[:, 1024 + (lc % 2) * 256: 1024 + (lc % 2 + 1) * 256]
                    self.wait_mark("scalar", "azmm")
                    self.wait_mark("scalar", f"azst{lc % 2}")
                    self.s(lambda e, zbank=zbank, zst=zst: e.activation(zst, zbank[:, 0:256], AF.Silu))
                    self.mark(f"azb{lc % 2}", "scalar", "sS")
                    self.mark("azsilu", "scalar", "sS")
                    xts = WS[:, lc * 1024 + sl * 256: lc * 1024 + (sl + 1) * 256]
                    t1 = self.SMA[:, 1536 + (lc % 2) * 256: 1536 + (lc % 2 + 1) * 256]
                    dsl = self.DROW[:, half * 1024 + sl * 256: half * 1024 + (sl + 1) * 256]
                    self.wait_mark("vector", "aymm")
                    self.wait_mark("vector", "azsilu")
                    self.v(lambda e, t1=t1, xts=xts, dsl=dsl: e.tensor_tensor(t1, xts, dsl, ALU.mult))
                    self.v(lambda e, t1=t1, ybank=ybank: e.tensor_tensor(t1, t1, ybank[:, 0:256], ALU.add))
                    self.v(lambda e, xts=xts, t1=t1, zst=zst: e.tensor_tensor(xts, t1, zst, ALU.mult))
                    self.mark(f"ayb{lc % 2}", "vector", "vS")
                    self.mark(f"azst{lc % 2}", "vector", "vS")
                self.mark("azw", "vector", "vS")
                if self.dbg == "slabbar":
                    self.p.barrier()
                if self.dbg == "z03" and half == 0 and sl == 3:
                    self.p.barrier(); self.dump_sbuf_bf16(self.WS[:, 8192:24576]); return
                if self.dbg == "z00" and half == 0 and sl == 0:
                    self.p.barrier(); self.dump_sbuf_bf16(self.WS[:, 16384:24576].rearrange("p (a b) -> p (a b)") if False else self.WS[:, 8192:24576]); return
            if self.dbg in ("yg0", "slabbar") and half == 0:
                self.p.barrier(); self.dump_sbuf_bf16(self.WS[:, 0:16384]); return
            self.wait_mark("sync", "awo")
            self.wait_mark("sync", "ainv")
            self.wait_mark("sync", "azmm")
            for dc in range(8):
                self.tile_dma("sync", self.WR[:, ts(dc, 1024)],
                              self.WoutTB[half * 1024:(half + 1) * 1024, ts(dc, 128)], "qWO")
            p.wait_cur("tensor", "qWO")
            for lt in range(4):
                for cc in range(8):
                    ptb = self.pt[cc % 2]
                    self.wait_mark("tensor", f"aptf{cc % 2}")
                    for k in range(4):
                        lc = lt * 4 + k
                        self.t(lambda e, lc=lc, cc=cc, k=k, ptb=ptb: e.transpose(
                            ptb[:, ts(k, 128)], WS[:, lc * 1024 + cc * 128: lc * 1024 + (cc + 1) * 128],
                            self.IDT[:, :]))
                    self.mark(f"aptm{cc % 2}", "tensor", "tS")
                    self.wait_mark("vector", f"aptm{cc % 2}")
                    self.v(lambda e, cc=cc, ptb=ptb: e.tensor_copy(
                        WS[:, Zb + cc * 512: Zb + (cc + 1) * 512], ptb[:, :]))
                    self.mark(f"aptf{cc % 2}", "vector", "vS")
                self.mark("aygT", "vector", "vS")
                self.wait_mark("tensor", "aygT")
                for dc in range(8):
                    bank = self.pb[dc % 2]
                    self.wait_mark("tensor", f"awb{dc % 2}")
                    for cc in range(8):
                        self.t(lambda e, dc=dc, cc=cc, bank=bank: e.matmul(
                            bank[:, :], self.WR[:, dc * 1024 + cc * 128: dc * 1024 + (cc + 1) * 128],
                            WS[:, Zb + cc * 512: Zb + (cc + 1) * 512],
                            start=(cc == 0), stop=(cc == 7)))
                    self.mark("awmm", "tensor", "tS")
                    self.wait_mark("vector", "awmm")
                    dstb = self.BIG[:, dc * 2048 + lt * 512: dc * 2048 + (lt + 1) * 512]
                    if half == 0:
                        self.v(lambda e, dstb=dstb, bank=bank: e.tensor_copy(dstb, bank[:, :]))
                    else:
                        self.v(lambda e, dstb=dstb, bank=bank: e.tensor_tensor(dstb, dstb, bank[:, :], ALU.add))
                    self.mark(f"awb{dc % 2}", "vector", "vS")
            self.mark("awo", "vector", "vS")
            self.mark("awr0", "tensor", "tS")
            self.mark("awr1", "tensor", "tS")
        self.residual_from_big(xsrc)
        p.barrier()

    def residual_from_big(self, xsrc, to_out=False):
        p = self.p
        p.barrier()
        dst = self.out if to_out else self.xres
        for cc in range(8):
            slot = cc % 2
            qn = f"qX{slot}"
            xr = self.SMA[:, :] if slot == 0 else self.SMB[:, :]
            self.wait_mark("sync", f"xrs{slot}")
            p.dma("sync", xr, xsrc[ts(cc, 128), :], qn)
            p.wait_cur("vector", qn)
            big = self.BIG[:, ts(cc, 2048)]
            p.wait_cur("vector", f"qRW{slot}")
            self.v(lambda e, big=big, xr=xr: e.tensor_tensor(big, big, xr, ALU.add))
            self.mark(f"xrs{slot}", "vector", "vS")
            self.mark(f"res{cc}", "vector", "vS")
            self.wait_mark("sync", f"res{cc}")
            p.dma("sync", dst[ts(cc, 128), :], big, f"qRW{slot}")
        p.barrier()

    def ph_ffn(self, xsrc):
        p = self.p
        self.v(lambda e: e.tensor_scalar(self.RA[:, :], self.RC[:, :], self.col(7)[0:1, :], None, ALU.mult))
        self.mark("ffgarow", "vector", "vS")
        self.wait_mark("tensor", "ffgarow")
        for qt in range(4):
            tsl = ts(qt, 512)
            bank = self.pb[5]
            self.wait_mark("tensor", "ffga")
            self.wait_mark("tensor", "ffb20")
            self.wait_mark("tensor", "ffb21")
            self.t(lambda e, qt=qt, bank=bank: e.matmul(
                bank[:, :], self.ONESF[0:1, :], self.RA[:, qt * 512:(qt + 1) * 512], start=True, stop=True))
            self.mark("ffgam", "tensor", "tS")
            self.wait_mark("vector", "ffgam")
            self.v(lambda e, bank=bank: e.tensor_copy(self.SMA[:, 0:512], bank[:, :]))
            self.mark("ffga", "vector", "vS")
            self.mark("ffga_v", "vector", "vS")
            acc = self.SMB[:, 1024:1536]
            p.wait_cur("vector", "qRW0")
            p.wait_cur("vector", "qRW1")
            for ic in range(32):
                prg = ic % 2
                self.wait_mark("sync", f"fwg{prg}")
                self.tile_dma("sync", self.WR[:, prg * 2048: prg * 2048 + 1024], self.wqg[:, ts(ic, 128)], f"qG{prg}")
                self.tile_dma("sync", self.WR[:, prg * 2048 + 1024: prg * 2048 + 2048], self.wqu[:, ts(ic, 128)], f"qG{prg}")
                p.wait_cur("tensor", f"qG{prg}")
                gb, ub = self.pb[ic % 2], self.pb[2 + ic % 2]
                self.wait_mark("tensor", f"ffg{ic % 2}")
                for kc in range(8):
                    self.t(lambda e, prg=prg, kc=kc, gb=gb, qt=qt: e.matmul(
                        gb[:, :], self.WR[:, prg * 2048 + kc * 128: prg * 2048 + (kc + 1) * 128],
                        self.XN[:, kc * 2048 + qt * 512: kc * 2048 + (qt + 1) * 512],
                        start=(kc == 0), stop=(kc == 7)))
                for kc in range(8):
                    self.t(lambda e, prg=prg, kc=kc, ub=ub, qt=qt: e.matmul(
                        ub[:, :], self.WR[:, prg * 2048 + 1024 + kc * 128: prg * 2048 + 1024 + (kc + 1) * 128],
                        self.XN[:, kc * 2048 + qt * 512: kc * 2048 + (qt + 1) * 512],
                        start=(kc == 0), stop=(kc == 7)))
                self.mark(f"fwg{prg}", "tensor", "tS")
                self.mark("ffmm", "tensor", "tS")
                self.wait_mark("vector", "ffmm")
                self.wait_mark("vector", "ffga_v")
                t1 = self.SMB[:, 0:512]
                self.v(lambda e, gb=gb, t1=t1: e.tensor_tensor(t1, gb[:, :], self.SMA[:, 0:512], ALU.mult))
                self.mark("fft1", "vector", "vS")
                self.wait_mark("scalar", "fft1")
                t2 = self.SMB[:, 512:1024]
                self.s(lambda e, t1=t1, t2=t2: e.activation(t2, t1, AF.Silu))
                self.mark("ffsilu", "scalar", "sS")
                self.wait_mark("vector", "ffsilu")
                hc = self.BIG[:, ts(ic, 512)]
                self.v(lambda e, hc=hc, t2=t2, ub=ub: e.tensor_tensor(hc, t2, ub[:, :], ALU.mult))
                self.mark("ffhc", "vector", "vS")
                self.wait_mark("scalar", "ffhc")
                self.s(lambda e, hc=hc, t2=t2: e.activation(t2, hc, AF.Abs))
                self.mark("ffabs", "scalar", "sS")
                self.wait_mark("vector", "ffabs")
                if ic == 0:
                    self.v(lambda e, t2=t2, acc=acc: e.tensor_copy(acc, t2))
                else:
                    self.v(lambda e, t2=t2, acc=acc: e.tensor_tensor(acc, acc, t2, ALU.max))
                self.mark(f"ffg{ic % 2}", "vector", "vS")
                self.mark("ffh", "vector", "vS")
            if self.dbg == "ffh0" and qt == 0:
                self.p.barrier(); self.dump_big(); return
            self.mark("fftree", "vector", "vS")
            self.wait_mark("tensor", "fftree")
            for tt in range(4):
                bank = self.pb[4 + tt % 2]
                self.wait_mark("tensor", f"fft{tt % 2}")
                self.t(lambda e, tt=tt, acc=acc, bank=bank: e.transpose(
                    bank[:, 0:128], acc[:, ts(tt, 128)], self.IDTF[:, :]))
                self.mark("fftr", "tensor", "tS")
                self.wait_mark("vector", "fftr")
                self.v(lambda e, tt=tt, bank=bank: e.tensor_reduce(
                    self.SMB[:, 1536 + tt: 1536 + tt + 1], bank[:, 0:128], AX.X, ALU.max, apply_absolute_value=True))
                self.mark(f"fft{tt % 2}", "vector", "vS")
            self.mark("ffmx", "vector", "vS")
            self.wait_mark("tensor", "ffmx")
            self.t(lambda e: e.transpose(self.pb[4][0:4, 0:128], self.SMB[:, 1536:1540], self.IDTF[:, :]))
            self.mark("ffmt", "tensor", "tS")
            self.wait_mark("vector", "ffmt")
            self.v(lambda e: e.tensor_copy(self.SMB[0:4, 1664:1792], self.pb[4][0:4, 0:128]))
            self.mark("ffmt2", "vector", "vS")
            self.wait_mark("sync", "ffmt2")
            p.dma("sync", self.rowscr.rearrange("o (n p) -> (o n) p", p=128)[0:4, :], self.SMB[0:4, 1664:1792], "qR")
            p.dma("sync", self.RB[:, 0:512], self.rowscr[:, 0:512], "qR")
            p.wait_cur("vector", "qR")
            self.v(lambda e: e.reciprocal(self.RB[:, 512:1024], self.RB[:, 0:512]))
            self.v(lambda e: e.tensor_scalar(self.RB[:, 512:1024], self.RB[:, 512:1024], 127.0, None, ALU.mult))
            self.v(lambda e, qt=qt: e.tensor_tensor(self.RB[:, 1024:1536], self.RB[:, 0:512], self.RC[:, qt * 512:(qt + 1) * 512], ALU.mult))
            self.v(lambda e: e.tensor_scalar(self.RB[:, 1024:1536], self.RB[:, 1024:1536], self.col(8)[0:1, :], self.col(9)[0:1, :], ALU.mult, ALU.mult))
            self.v(lambda e: e.tensor_scalar(self.RB[:, 1024:1536], self.RB[:, 1024:1536], 1.0 / 127.0, None, ALU.mult))
            self.mark("ffrow2", "vector", "vS")
            self.wait_mark("tensor", "ffrow2")
            for bi, off in ((0, 512), (1, 1024)):
                bank = self.pb[4 + bi]
                self.wait_mark("tensor", f"ffb2{bi}")
                self.t(lambda e, off=off, bank=bank: e.matmul(
                    bank[:, :], self.ONESF[0:1, :], self.RB[:, off:off + 512], start=True, stop=True))
                self.mark("ffbc2", "tensor", "tS")
                self.wait_mark("vector", "ffbc2")
                dst = self.SMA[:, 512 + bi * 512: 1024 + bi * 512]
                self.v(lambda e, dst=dst, bank=bank: e.tensor_copy(dst, bank[:, :]))
                self.mark(f"ffb2{bi}", "vector", "vS")
            self.mark("ffs2", "vector", "vS")
            self.wait_mark("vector", "ffs2")
            for ic in range(32):
                hc = self.BIG[:, ts(ic, 512)]
                t1 = self.SMB[:, 0:512]
                self.v(lambda e, hc=hc, t1=t1: e.tensor_tensor(t1, hc, self.SMA[:, 512:1024], ALU.mult))
                self.round_into(self.WS[:, ts(ic, 512)], t1)
            self.mark("ffq2", "vector", "vS")
            if self.dbg == "ffq0" and qt == 0:
                self.p.barrier(); self.dump_sbuf_bf16(self.WS[:, 0:16384]); return
            self.wait_mark("tensor", "ffq2")
            for dc in range(8):
                prd = dc % 2
                self.wait_mark("sync", f"fwd{prd}")
                self.tile_dma("sync", self.WR[:, prd * 4096: prd * 4096 + 4096], self.wqd[:, ts(dc, 128)], f"qD{prd}")
                p.wait_cur("tensor", f"qD{prd}")
                bank = self.pb[dc % 2]
                self.wait_mark("tensor", f"ffd{dc % 2}")
                for kc in range(32):
                    self.t(lambda e, prd=prd, kc=kc, bank=bank: e.matmul(
                        bank[:, :], self.WR[:, prd * 4096 + kc * 128: prd * 4096 + (kc + 1) * 128],
                        self.WS[:, ts(kc, 512)], start=(kc == 0), stop=(kc == 31)))
                self.mark(f"fwd{prd}", "tensor", "tS")
                self.mark("ffdm", "tensor", "tS")
                slot = dc % 2
                qn = f"qX{slot}"
                xr = self.SMA[:, 1536 + slot * 256: 1536 + slot * 256 + 512][:, 0:512] if False else self.SMB[:, 1536:2048] if slot == 0 else self.SMA[:, 0:512]
                self.wait_mark("sync", f"xrf{slot}")
                p.dma("sync", xr, xsrc[ts(dc, 128), tsl], qn)
                p.wait_cur("vector", qn)
                p.wait_cur("vector", f"qRW{slot}")
                self.wait_mark("vector", "ffdm")
                t1 = self.BIG[:, slot * 512:(slot + 1) * 512]
                self.v(lambda e, t1=t1, bank=bank: e.tensor_tensor(t1, bank[:, :], self.SMA[:, 1024:1536], ALU.mult))
                self.mark(f"ffd{dc % 2}", "vector", "vS")
                self.v(lambda e, t1=t1, xr=xr: e.tensor_tensor(t1, t1, xr, ALU.add))
                self.mark(f"xrf{slot}", "vector", "vS")
                self.mark(f"ffrs{dc}", "vector", "vS")
                self.wait_mark("sync", f"ffrs{dc}")
                p.dma("sync", self.xres[ts(dc, 128), tsl], t1, f"qRW{slot}")
        p.barrier()

    def ph_attn_b(self, xsrc):
        p = self.p
        WS = self.WS
        QK, VT, QI, KI, VP, UU, EB, ZBT = 0, 4096, 6144, 10240, 14336, 16448, 20544, 21568
        self.v(lambda e: e.memset(WS[64:128, UU:UU + 4096], 0.0))
        self.v(lambda e: e.memset(WS[64:128, QI:QI + 4096], 0.0))
        self.v(lambda e: e.memset(WS[64:128, KI:KI + 4096], 0.0))
        for rgn, asrc in ((QI, self.AUGQ), (KI, self.AUGK)):
            for rep in range(32):
                self.v(lambda e, rgn=rgn, asrc=asrc, rep=rep: e.tensor_copy(
                    WS[64:72, rgn + rep * 128: rgn + (rep + 1) * 128], asrc[:, :]))
        self.mark("abinit", "vector", "vS")
        for tc in range(8):
            tbase = tc * 256
            for fc in range(16):
                prq = fc % 2
                self.wait_mark("sync", f"bqw{prq}")
                self.tile_dma("sync", self.WR[:, prq * 1024: (prq + 1) * 1024], self.WqkTB[:, ts(fc, 128)], f"qQ{prq}")
                p.wait_cur("tensor", f"qQ{prq}")
                bank = self.pb[fc % 2]
                self.wait_mark("tensor", f"bqb{fc % 2}")
                self.wait_mark("tensor", f"bpb{fc % 2}")
                for kc in range(8):
                    self.t(lambda e, prq=prq, kc=kc, bank=bank, tbase=tbase: e.matmul(
                        bank[:, 0:256], self.WR[:, prq * 1024 + kc * 128: prq * 1024 + (kc + 1) * 128],
                        self.XN[:, kc * 2048 + tbase: kc * 2048 + tbase + 256],
                        start=(kc == 0), stop=(kc == 7)))
                self.mark(f"bqw{prq}", "tensor", "tS")
                self.mark("bqmm", "tensor", "tS")
                self.wait_mark("scalar", "bqmm")
                self.wait_mark("scalar", "bile_done")
                self.s(lambda e, fc=fc, bank=bank: e.copy(WS[:, QK + fc * 256: QK + (fc + 1) * 256], bank[:, 0:256]))
                self.mark(f"bqb{fc % 2}", "scalar", "sS")
            self.mark("bqk", "scalar", "sS")
            self.wait_mark("sync", "bvw")
            self.wait_mark("sync", "bqw0")
            self.wait_mark("sync", "bqw1")
            for kc in range(8):
                srcv = self.WvTB[ts(kc, 128), :]
                p.dma("sync", self.WR[:, kc * 1024:(kc + 1) * 1024], srcv, "qV")
            p.wait_cur("tensor", "qV")
            for tt in range(2):
                for nt in range(2):
                    bank = self.pb[2 + (tt * 2 + nt) % 2]
                    self.wait_mark("tensor", f"bvb{(tt * 2 + nt) % 2}")
                    for kc in range(8):
                        self.t(lambda e, kc=kc, nt=nt, tt=tt, bank=bank, tbase=tbase: e.matmul(
                            bank[:, :], self.XN[:, kc * 2048 + tbase + tt * 128: kc * 2048 + tbase + (tt + 1) * 128],
                            self.WR[:, kc * 1024 + nt * 512: kc * 1024 + (nt + 1) * 512],
                            start=(kc == 0), stop=(kc == 7)))
                    self.mark("bvmm", "tensor", "tS")
                    self.wait_mark("scalar", "bvmm")
                    p.wait_cur("scalar", "qVS")
                    self.s(lambda e, tt=tt, nt=nt, bank=bank: e.copy(
                        WS[:, VT + tt * 1024 + nt * 512: VT + tt * 1024 + (nt + 1) * 512], bank[:, :]))
                    self.mark(f"bvb{(tt * 2 + nt) % 2}", "scalar", "sS")
            self.mark("bvw", "scalar", "sS")
            self.mark("bv", "scalar", "sS")
            self.wait_mark("sync", "bv")
            p.wait_cur("sync", "qVS")
            p.dma("sync", self.vscr[0:128, :], WS[:, VT:VT + 1024], "qVS")
            p.dma("sync", self.vscr[128:256, :], WS[:, VT + 1024:VT + 2048], "qVS")
            p.wait_cur("sync", "qVS")
            p.wait_cur("gpsimd", "qVS")
            p.wait_cur("gpsimd", "qVSg")
            self.wait_mark("sync", "bvpg_used")
            self.wait_mark("gpsimd", "bvpg_used")
            for g in range(16):
                srcp = self.vscr[:, g * 64:(g + 1) * 64].rearrange("(grp pos) dd -> pos grp dd", pos=8)
                dstp = WS[g * 8:(g + 1) * 8, VP:VP + 2080].rearrange("p (grp w) -> p grp w", w=65)[:, :, 0:64]
                p.dma("sync", dstp, srcp, "qVS") if g % 2 == 0 else p.dma("gpsimd", dstp, srcp, "qVSg")
            self.v(lambda e: e.memset(WS[:, VP:VP + 2080].rearrange("p (grp w) -> p grp w", w=65)[:, :, 64:65], 1.0))
            self.mark("bvones", "vector", "vS")
            self.wait_mark("vector", "bqk")
            self.wait_mark("vector", "bile_used")
            for h in range(16):
                srcq = WS[(h % 2) * 64:(h % 2) * 64 + 64, QK + (h // 2) * 256: QK + (h // 2 + 1) * 256].rearrange(
                    "p (grp pos) -> p grp pos", pos=8)
                dstq = WS[0:64, QI:QI + 4096].rearrange("p (grp cpos) -> p grp cpos", cpos=128)[:, :, h * 8:(h + 1) * 8]
                self.v(lambda e, dstq=dstq, srcq=srcq: e.tensor_copy(dstq, srcq))
                srck = WS[(h % 2) * 64:(h % 2) * 64 + 64, QK + 2048 + (h // 2) * 256: QK + 2048 + (h // 2 + 1) * 256].rearrange(
                    "p (grp pos) -> p grp pos", pos=8)
                dstk = WS[0:64, KI:KI + 4096].rearrange("p (grp cpos) -> p grp cpos", cpos=128)[:, :, h * 8:(h + 1) * 8]
                self.v(lambda e, dstk=dstk, srck=srck: e.tensor_copy(dstk, srck))
            self.mark("bile", "vector", "vS")
            self.mark("bile_done", "vector", "vS")
            p.wait_cur("tensor", "qVS")
            p.wait_cur("tensor", "qVSg")
            self.wait_mark("tensor", "bile")
            self.wait_mark("tensor", "bvones")
            for gb in range(8):
                sbank = self.pb[4 + gb % 2]
                self.wait_mark("tensor", f"bsb{gb % 2}")
                for k in range(4):
                    grp = gb * 4 + k
                    self.t(lambda e, grp=grp, k=k, sbank=sbank: e.matmul(
                        sbank[:, ts(k, 128)], WS[0:72, KI + grp * 128: KI + (grp + 1) * 128],
                        WS[0:72, QI + grp * 128: QI + (grp + 1) * 128], start=True, stop=True))
                self.mark("bsmm", "tensor", "tS")
                eslot = gb % 2
                self.wait_mark("scalar", "bsmm")
                self.wait_mark("scalar", f"beb{eslot}")
                self.s(lambda e, eslot=eslot, sbank=sbank: e.activation(
                    WS[:, EB + eslot * 512: EB + (eslot + 1) * 512], sbank[:, :], AF.Exp, scale=0.125))
                self.mark(f"bsb{gb % 2}", "scalar", "sS")
                self.mark("bexp", "scalar", "sS")
                ubank = self.pb[2 + gb % 2]
                self.wait_mark("tensor", "bexp")
                self.wait_mark("tensor", f"bub{gb % 2}")
                for k in range(4):
                    grp = gb * 4 + k
                    self.t(lambda e, grp=grp, k=k, ubank=ubank, eslot=eslot: e.matmul(
                        ubank[0:65, ts(k, 128)], WS[:, VP + grp * 65: VP + (grp + 1) * 65],
                        WS[:, EB + eslot * 512 + k * 128: EB + eslot * 512 + (k + 1) * 128],
                        start=True, stop=True))
                self.mark("bumm", "tensor", "tS")
                self.mark(f"beb{eslot}", "tensor", "tS")
                self.wait_mark("vector", "bumm")
                self.wait_mark("vector", "bu_used")
                self.v(lambda e, gb=gb, ubank=ubank: e.tensor_copy(
                    WS[0:65, UU + gb * 512: UU + (gb + 1) * 512], ubank[0:65, :]))
                self.mark(f"bub{gb % 2}", "vector", "vS")
            self.mark("bvpg_used", "vector", "vS")
            self.mark("bile_used", "vector", "vS")
            self.mark("bU", "vector", "vS")
            for zh in range(2):
                self.wait_mark("vector", "bzmm_last")
                self.v(lambda e, zh=zh: e.reciprocal(self.RA[:, :], WS[64:65, UU + zh * 2048: UU + (zh + 1) * 2048]))
                self.mark("bzr", "vector", "vS")
                self.wait_mark("tensor", "bzr")
                for nth in range(4):
                    nt = zh * 4 + nth
                    bank = self.pb[4 + nt % 2]
                    self.wait_mark("tensor", f"bzb{nt % 2}")
                    self.t(lambda e, nth=nth, bank=bank: e.matmul(
                        bank[:, :], self.ONESF[0:1, :], self.RA[:, nth * 512:(nth + 1) * 512],
                        start=True, stop=True))
                    self.mark("bzmm", "tensor", "tS")
                    self.wait_mark("scalar", "bzmm")
                    self.wait_mark("scalar", "bztt")
                    self.s(lambda e, bank=bank: e.copy(WS[:, ZBT:ZBT + 512], bank[:, :]))
                    self.mark("bzcp", "scalar", "sS")
                    self.wait_mark("vector", "bzcp")
                    self.v(lambda e, nt=nt: e.tensor_tensor(
                        WS[0:64, UU + nt * 512: UU + (nt + 1) * 512],
                        WS[0:64, UU + nt * 512: UU + (nt + 1) * 512], WS[0:64, ZBT:ZBT + 512], ALU.mult))
                    self.mark(f"bzb{nt % 2}", "vector", "vS")
                    self.mark("bztt", "vector", "vS")
                self.mark("bzmm_last", "tensor", "tS")
            self.mark("bUn", "vector", "vS")
            if self.dbg == "bu0" and tc == 0:
                self.p.barrier(); self.dump4k(self.WS[:, 16448:16448 + 4096]); return
            self.wait_mark("tensor", "bUn")
            for dc in range(8):
                prd = dc % 2
                self.wait_mark("sync", f"bpw{prd}")
                self.tile_dma("sync", self.WR[:, 4096 + prd * 2048: 4096 + (prd + 1) * 2048], self.WprojTpB[:, ts(dc, 128)], f"qP{prd}")
                p.wait_cur("tensor", f"qP{prd}")
                bank = self.pb[dc % 2]
                self.wait_mark("tensor", f"bpb{dc % 2}")
                for h in range(16):
                    rhs = WS[:, UU:UU + 4096].rearrange("p (grp hh pos) -> p grp hh pos", hh=16, pos=8)[:, :, h, :]
                    self.t(lambda e, h=h, prd=prd, bank=bank, rhs=rhs: e.matmul(
                        bank[:, 0:256], self.WR[:, 4096 + prd * 2048 + h * 128: 4096 + prd * 2048 + (h + 1) * 128],
                        rhs, start=(h == 0), stop=(h == 15)))
                self.mark(f"bpw{prd}", "tensor", "tS")
                self.mark("bpmm", "tensor", "tS")
                slot = dc % 2
                qn = f"qX{slot}"
                xr = self.SMA[:, slot * 256:(slot + 1) * 256]
                self.wait_mark("sync", f"xrb{slot}")
                p.dma("sync", xr, xsrc[ts(dc, 128), tbase:tbase + 256], qn)
                p.wait_cur("vector", qn)
                self.wait_mark("vector", "bpmm")
                p.wait_cur("vector", f"qRW{slot}")
                t1 = self.SMB[:, slot * 256:(slot + 1) * 256]
                self.v(lambda e, t1=t1, bank=bank, dc=dc, xr=xr: e.scalar_tensor_tensor(
                    t1, bank[:, 0:256], self.BPC[:, dc:dc + 1], xr, ALU.add, ALU.add))
                self.mark(f"bpb{dc % 2}", "vector", "vS")
                self.mark(f"xrb{slot}", "vector", "vS")
                self.mark(f"brs{dc}", "vector", "vS")
                self.wait_mark("sync", f"brs{dc}")
                p.dma("sync", self.xres[ts(dc, 128), tbase:tbase + 256], t1, f"qRW{slot}")
                if self.dbg == "bxr0" and tc == 0 and dc == 0:
                    p.dma("sync", self.out[0:128, 0:256], xr, "qO0")
                    p.dma("sync", self.out[0:128, 256:512], t1, "qO0")
                    p.wait_cur("sync", "qO0")
                    self.p.barrier()
                    self._stop = True
                    return
                if self.dbg == "bxr1" and tc == 0 and dc == 0:
                    p.dma("sync", self.dbgscr[:, 0:256], xr, "qDB")
                    p.dma("sync", self.dbgscr[:, 256:512], t1, "qDB")
                    p.wait_cur("sync", "qDB")
            self.mark("bu_used", "vector", "vS")
        p.barrier()

    def dump4k(self, ap4096):
        p = self.p
        for hh in range(2):
            dst = self.SMA if hh == 0 else self.SMB
            self.v(lambda e, hh=hh, dst=dst: e.tensor_copy(dst[:, :], ap4096[:, hh * 2048:(hh + 1) * 2048]))
            self.mark(f"d4k{hh}", "vector", "vS")
            self.wait_mark("sync", f"d4k{hh}")
            p.dma("sync", self.out[ts(hh, 128), :], dst[:, :], "qO0")
        p.wait_cur("sync", "qO0")
        p.barrier()

    def ph_finale(self, xsrc):
        p = self.p
        for cc in range(8):
            slot = cc % 2
            qn = f"qX{slot}"
            xr = self.SMA[:, :] if slot == 0 else self.SMB[:, :]
            p.wait_cur("sync", f"qO{slot}")
            p.dma("sync", xr, xsrc[ts(cc, 128), :], qn)
            p.wait_cur("sync", qn)
            p.dma("sync", self.out[ts(cc, 128), :], xr, f"qO{slot}")
        p.wait_cur("sync", "qO0")
        p.wait_cur("sync", "qO1")
        p.barrier()

    def dump_sbuf_bf16(self, ap16384):
        p = self.p
        for cc in range(8):
            slot = cc % 2
            dst = self.SMA if slot == 0 else self.SMB
            p.wait_cur("vector", f"qO{slot}")
            self.v(lambda e, cc=cc, dst=dst: e.tensor_copy(dst[:, :], ap16384[:, ts(cc, 2048)]))
            self.mark(f"dmp{cc}", "vector", "vS")
            self.wait_mark("sync", f"dmp{cc}")
            p.dma("sync", self.out[ts(cc, 128), :], dst[:, :], f"qO{slot}")
        p.wait_cur("sync", "qO0")
        p.wait_cur("sync", "qO1")
        p.barrier()

    def dump_dram(self, src_ap):
        p = self.p
        p.dma("sync", self.out, src_ap, "qO0")
        p.wait_cur("sync", "qO0")
        p.barrier()

    def dump_big(self):
        p = self.p
        for cc in range(8):
            self.mark(f"dmpb{cc}", "vector", "vS")
            self.wait_mark("sync", f"dmpb{cc}")
            p.wait_cur("sync", "qO0")
            p.dma("sync", self.out[ts(cc, 128), :], self.BIG[:, ts(cc, 2048)], "qO0")
        p.wait_cur("sync", "qO0")
        p.barrier()

    def build(self):
        self.ph_init()
        self.ph_ternarize()
        if self.dbg == "wq":
            self.p.dma("gpsimd", self.out, self.wqg[:, 0:2048], "qO0")
            self.p.wait_cur("gpsimd", "qO0")
            self.p.barrier()
            return
        self.ph_kf()
        if self.dbg == "kf":
            self.dump_dram(self.kfB[0:1024, 0:2048]); return
        xsrc = self.x0
        if self.n_sub >= 1:
            self.ph_ln_std(xsrc, 1)
            if self.dbg == "xn1":
                self.dump_sbuf_bf16(self.XN); return
            self.ph_attn_a(xsrc)
            if self.dbg in ("xt0", "yg0", "z00", "z03", "slabbar"):
                return
            if self.dbg == "aout":
                self.dump_big(); return
            xsrc = self.xres
        if self.n_sub >= 2:
            self.ph_ln_quant(xsrc, 2)
            if self.dbg == "xq2":
                self.dump_sbuf_bf16(self.XN); return
            self.ph_ffn(xsrc)
            if self.dbg in ("ffh0", "ffq0"):
                return
        if self.n_sub >= 3:
            self.ph_ln_std(xsrc, 3)
            if self.dbg == "xn3":
                self.dump_sbuf_bf16(self.XN); return
            self.ph_attn_b(xsrc)
            if self.dbg in ("bu0", "bxr0"):
                return
        if self.n_sub >= 4:
            self.ph_ln_quant(xsrc, 4)
            self.ph_ffn(xsrc)
        self.ph_finale(xsrc)
        if self.dbg == "bxr1":
            self.p.dma("sync", self.out[0:128, 0:512], self.dbgscr, "qO0")
            self.p.wait_cur("sync", "qO0")
            self.p.barrier()


def host_prep(inputs):
    f32 = np.float32
    W_in = inputs["L0_attn_a_W_in"]
    d = {}
    d["WinT"] = np.ascontiguousarray(W_in.T).astype(f32, copy=False)
    cw = inputs["L0_attn_a_conv_w"]
    d["conv_wh"] = np.ascontiguousarray(cw.reshape(16, 128, 4).transpose(1, 0, 2).reshape(128, 64))
    d["conv_bh"] = np.ascontiguousarray(inputs["L0_attn_a_conv_b"].reshape(16, 128).T)
    d["k_ssm"] = np.ascontiguousarray(inputs["L0_attn_a_k_ssm"])
    d["D_skip"] = np.ascontiguousarray(inputs["L0_attn_a_D_skip"].reshape(1, C))
    d["WoutT"] = np.ascontiguousarray(inputs["L0_attn_a_W_out"].T)
    Wqkv = inputs["L0_attn_b_Wqkv"]
    W3 = Wqkv.reshape(16, 3, 64, D)
    q = W3[:, 0].reshape(1024, D)
    k = W3[:, 1].reshape(1024, D)
    v = W3[:, 2].reshape(1024, D)
    d["WqkT"] = np.ascontiguousarray(np.concatenate([q, k], 0).T)
    d["WvT"] = np.ascontiguousarray(v.T)
    Wproj = inputs["L0_attn_b_Wproj"]
    wpp = np.zeros((16, 128, D), f32)
    for h in range(16):
        wpp[h, 0:64, :] = Wproj[:, h * 64:(h + 1) * 64].T
    d["WprojTp"] = wpp.reshape(16 * 128, D)
    d["bprojh"] = np.ascontiguousarray(inputs["L0_attn_b_bproj"].reshape(8, 128).T)
    d["wgT"] = np.ascontiguousarray(inputs["L0_ffn_w_gate"].T)
    d["wuT"] = np.ascontiguousarray(inputs["L0_ffn_w_up"].T)
    d["wdT"] = np.ascontiguousarray(inputs["L0_ffn_w_down"].T)
    lv = np.arange(NFFT, dtype=np.float64)[:, None]
    fv = np.arange(NF, dtype=np.float64)[None, :]
    ang = 2.0 * np.pi * lv * fv / NFFT
    Cfm = np.cos(ang)
    Sfm = -np.sin(ang)
    Cfm[:, 2049:] = 0.0
    Sfm[:, 2049:] = 0.0
    Sfm[:, 2048] = 0.0
    d["Cf"] = Cfm.astype(f32)
    d["Sf"] = Sfm.astype(f32)
    fv2 = np.arange(NF, dtype=np.float64)[:, None]
    lv2 = np.arange(L, dtype=np.float64)[None, :]
    ang2 = 2.0 * np.pi * fv2 * lv2 / NFFT
    wgt = np.full((NF, 1), 2.0 / NFFT)
    wgt[0] = 1.0 / NFFT
    wgt[2048] = 1.0 / NFFT
    wgt[2049:] = 0.0
    Cim = wgt * np.cos(ang2)
    Sim = -wgt * np.sin(ang2)
    Sim[2048:, :] = 0.0
    d["Ci"] = Cim.astype(f32)
    d["Si"] = Sim.astype(f32)
    d["ones_h"] = np.ones((128, 128), f32)
    d["ident_h"] = np.eye(128, dtype=f32)
    aq = np.zeros((8, 128), f32)
    ak = np.zeros((8, 128), f32)
    for c in range(16):
        for pos in range(8):
            cix = c * 8 + pos
            for pp in range(8):
                aq[pp, cix] = -600.0 if pp == pos else 0.0
                ak[pp, cix] = 0.0 if pp == pos else 1.0
    d["augq_h"] = aq
    d["augk_h"] = ak
    return d


def build_program(n_sub=4, dbg=None):
    from contextlib import ExitStack
    stack = ExitStack()
    nc = bass.Bass("TRN2", target_bir_lowering=False)
    mk = MK(nc, stack, n_sub=n_sub, dbg=dbg)
    mk.build()
    mk.p.emit(nc, stack)
    stack.close()
    return nc


def run_cores(inputs, n_sub=4, cores=(0,), x_override=None, dbg=None):
    from concourse.bass_utils import run_bass_kernel_spmd
    hs = inputs["L0_attn_a_hidden_states"]
    hd = host_prep(inputs)
    nc = build_program(n_sub=n_sub, dbg=dbg)
    in_maps = []
    for b in cores:
        m = dict(hd)
        xb = hs[b] if x_override is None else x_override[b]
        m["x0"] = np.ascontiguousarray(xb.T.astype(np.float32))
        in_maps.append(m)
    res = run_bass_kernel_spmd(nc, in_maps, core_ids=list(range(len(cores))))
    outs = [np.ascontiguousarray(r["out"].T) for r in res.results]
    return outs




def kernel(**inputs):
    from concourse.bass_utils import run_bass_kernel_spmd
    hs = np.asarray(inputs["L0_attn_a_hidden_states"])
    hd = host_prep({k: np.asarray(v) for k, v in inputs.items()})
    nc = build_program(n_sub=4)
    in_maps = []
    for b in range(8):
        m = dict(hd)
        m["x0"] = np.ascontiguousarray(hs[b].T.astype(np.float32, copy=False))
        in_maps.append(m)
    res = run_bass_kernel_spmd(nc, in_maps, core_ids=list(range(8)))
    out = np.stack([np.ascontiguousarray(np.asarray(r["out"]).T) for r in res.results], axis=0)
    return out.astype(np.float32, copy=False)
